# version 43
# baseline (speedup 1.0000x reference)
import numpy as np
import ml_dtypes
import concourse.bass as bass
import concourse.mybir as mybir
from concourse.bass_utils import run_bass_kernel_spmd

F32 = mybir.dt.float32
BF16 = mybir.dt.bfloat16
AF = mybir.ActivationFunctionType
ALU = mybir.AluOpType
AX = mybir.AxisListType

NS = 512


def _slices(total, width=NS):
    return [(i * width, min(width, total - i * width))
            for i in range((total + width - 1) // width)]


class Plan:
    ENGINES = ("sync", "scalar", "vector", "tensor", "gpsimd")
    ENG_SEM = {"sync": "e_sync", "scalar": "e_act", "vector": "e_dve",
               "tensor": "e_pe", "gpsimd": "e_pool"}

    def __init__(self):
        self.ops = {e: [] for e in self.ENGINES}
        self.sem_count = {}
        self.res = {}
        self.touched = set()
        self.hazard = []
        self.waited = {e: {} for e in self.ENGINES}

    def emit(self, eng, fn, reads=(), writes=(), dma=False, dsem=None, fresh_ok=False):
        need = {}

        def add(tok):
            sem, val = tok
            if need.get(sem, 0) < val:
                need[sem] = val

        for r in reads:
            e = self.res.get(r)
            if e is None:
                continue
            if e["w"] is not None:
                add(e["w"][1])
            if r.startswith("@"):
                for (re_, rt) in e["r"]:
                    if re_ != eng:
                        add(rt)
        for w in writes:
            e = self.res.get(w)
            if e is None:
                if not fresh_ok:
                    for t in self.hazard:
                        add(t)
                continue
            if e["w"] is not None:
                add(e["w"][1])
            for (_, rt) in e["r"]:
                add(rt)
        for sem in sorted(need):
            val = need[sem]
            if self.waited[eng].get(sem, 0) < val:
                self.ops[eng].append(("wait", sem, val))
                self.waited[eng][sem] = val
        if dma:
            assert dsem is not None, "dma emits need a per-stream dsem"
            sem, inc = "d_" + dsem, 16
        else:
            sem, inc = self.ENG_SEM[eng], 1
        val = self.sem_count.get(sem, 0) + inc
        self.sem_count[sem] = val
        tok = (sem, val)
        self.ops[eng].append(("call", fn, sem, inc))
        for r in reads:
            self.res.setdefault(r, {"w": None, "r": []})["r"].append((eng, tok))
        for w in writes:
            self.res[w] = {"w": (eng, tok), "r": []}
        self.touched.update(reads)
        self.touched.update(writes)
        return tok

    def settle(self, keys, tok):
        for k in keys:
            e = self.res.setdefault(k, {"w": None, "r": []})
            e["w"] = ("dma", tok)

    def wait(self, eng, tok):
        sem, val = tok
        if self.waited[eng].get(sem, 0) < val:
            self.ops[eng].append(("wait", sem, val))
            self.waited[eng][sem] = val

    def replay(self, nc, stack):
        names = set(self.ENG_SEM.values())
        for ops in self.ops.values():
            for op in ops:
                names.add(op[1] if op[0] == "wait" else op[2])
        sems = {}
        for name in sorted(names):
            sems[name] = stack.enter_context(nc.semaphore(name))
        block = stack.enter_context(nc.Block())

        def mk(engname):
            def body(engine):
                for op in self.ops[engname]:
                    if op[0] == "wait":
                        engine.wait_ge(sems[op[1]], op[2])
                    else:
                        _, fn, sem, inc = op
                        fn(engine).then_inc(sems[sem], inc)
            return body

        for engname in self.ENGINES:
            if self.ops[engname]:
                getattr(block, engname)(mk(engname))


class Arena:

    def __init__(self, nc, stack, cols_f32, plan):
        self.t = stack.enter_context(nc.sbuf_tensor("arena", [128, cols_f32], F32))
        self.tb = self.t[:].bitcast(BF16)
        self.cols = cols_f32
        self.plan = plan
        self.off = 0
        self.peak = 0
        self.marks = []
        self.live = []
        self.hazards = []

    def alloc(self, key, cols, dtype):
        f32cols = cols if dtype == F32 else (cols + 1) // 2
        start = self.off
        self.off += f32cols
        self.peak = max(self.peak, self.off)
        assert self.off <= self.cols, f"arena overflow at {key}: {self.off} > {self.cols}"
        if key is not None:
            self.live.append(key)
        if dtype == F32:
            return self.t[:, start:start + cols]
        return self.tb[:, 2 * start:2 * start + cols]

    def alloc_top(self, key, cols, dtype):
        f32cols = cols if dtype == F32 else (cols + 1) // 2
        self.cols -= f32cols
        start = self.cols
        assert self.off <= self.cols, f"arena top overflow at {key}"
        if dtype == F32:
            return self.t[:, start:start + cols]
        return self.tb[:, 2 * start:2 * start + cols]

    def push(self):
        self.marks.append((self.off, self.live, set(self.plan.touched)))
        self.live = []

    def pop(self):
        off, live, snap = self.marks.pop()
        toks = []
        for k in self.plan.touched - snap:
            e = self.plan.res.get(k)
            if e:
                if e["w"] is not None:
                    toks.append(e["w"][1])
                toks += [t for (_, t) in e["r"]]
        best = {}
        for (s, v) in toks + self.hazards:
            if best.get(s, 0) < v:
                best[s] = v
        self.hazards = sorted(best.items())
        self.plan.hazard = list(self.hazards)
        self.off, self.live = off, live


def build(D=2048, S=2048, SO=1024, H=16, E=8, R=16):
    dh = 128
    assert H * dh == D and D % 128 == 0 and S % 128 == 0 and SO % 128 == 0
    nD = D // 128
    nT = S // 128
    ER = E * R
    SCALE = float(dh) ** -0.5
    SHIFT = -24.0
    EPS = 1e-5
    half = S // 2
    assert half <= 1024 and SO <= 1024

    nc = bass.Bass("TRN2", target_bir_lowering=False, debug=False)
    x_d = nc.dram_tensor("x", [D, S], F32, kind="ExternalInput").ap()
    wq_d = nc.dram_tensor("wq", [D, D], F32, kind="ExternalInput").ap()
    wk_d = nc.dram_tensor("wk", [D, D], F32, kind="ExternalInput").ap()
    wv_d = nc.dram_tensor("wv", [D, D], F32, kind="ExternalInput").ap()
    wo_d = nc.dram_tensor("wo", [D, D], F32, kind="ExternalInput").ap()
    wb_d = nc.dram_tensor("wb", [D, D], F32, kind="ExternalInput").ap()
    wr_d = nc.dram_tensor("wr", [128, nD * E], F32, kind="ExternalInput").ap()
    at_d = nc.dram_tensor("at", [D, ER], F32, kind="ExternalInput").ap()
    bw_d = nc.dram_tensor("bw", [ER, D], F32, kind="ExternalInput").ap()
    bb_d = nc.dram_tensor("bb", [128, nD], F32, kind="ExternalInput").ap()
    r2_d = nc.dram_tensor("r2", [E, ER], F32, kind="ExternalInput").ap()
    eye_d = nc.dram_tensor("eye", [128, 128], BF16, kind="ExternalInput").ap()
    y_d = nc.dram_tensor("y", [D, SO], F32, kind="ExternalOutput").ap()

    p = Plan()
    from contextlib import ExitStack
    with ExitStack() as top:
        EC = top.enter_context
        consts = EC(nc.sbuf_tensor("consts", [128, 8], F32))
        ones_c = EC(nc.sbuf_tensor("ones_c", [128, 1], BF16))
        ones_r = EC(nc.sbuf_tensor("ones_r", [1, 128], BF16))
        eye_s = EC(nc.sbuf_tensor("eye_s", [128, 128], BF16))
        r2_s = EC(nc.sbuf_tensor("r2_s", [E, ER], BF16))
        wr_s = EC(nc.sbuf_tensor("wr_s", [128, nD * E], BF16))
        bb_s = EC(nc.sbuf_tensor("bb_s", [128, nD], F32))
        bw_s = EC(nc.sbuf_tensor("bw_s", [ER, D], BF16))
        xn_s = EC(nc.sbuf_tensor("xn_s", [128, nD * S], BF16))
        attn_s = EC(nc.sbuf_tensor("attn_s", [128, H * SO], BF16))
        mask16 = EC(nc.sbuf_tensor("mask16", [128, SO], BF16))
        PA = EC(nc.psum_tensor("PA", [128, 1024], F32))
        PB = EC(nc.psum_tensor("PB", [128, 1024], F32))
        PC = EC(nc.psum_tensor("PC", [128, 1024], F32))
        PD = EC(nc.psum_tensor("PD", [128, 1024], F32))
        PCb = PC[:].bitcast(BF16)
        PAb = PA[:].bitcast(BF16)

        ar = Arena(nc, top, 26400, p)
        W = nD * dh
        wq_s = ar.alloc_top("wq_b", 2 * W, BF16)
        wk_s = ar.alloc_top("wk_b", 2 * W, BF16)
        wv_s = ar.alloc_top("wv_b", 2 * W, BF16)
        sum_bf = ar.alloc("sum_b", 2 * SO, BF16)
        srowb = ar.alloc("srowB", SO, F32)
        browb = ar.alloc("browB", SO, BF16)
        recipb = ar.alloc("rcb_b", SO, BF16)
        lg_s = ar.alloc("lgs", (SO // 128) * E, F32)
        mx_s = ar.alloc("mxs", SO // 128, F32)
        mtm_s = ar.alloc("mtms", (SO // 128) * E, BF16)
        mfm_s = ar.alloc("mfms", SO, BF16)

        CK = {"zero": 0, "shift": 1, "invD": 2, "eps": 3, "neghalf": 4,
              "one": 5, "scale": 6, "mone": 7}

        def ck(name, rows=128):
            i = CK[name]
            return consts[0:rows, i:i + 1]

        def xn(c):
            return xn_s[:, c * S:(c + 1) * S]

        def setup_consts(v):
            v.memset(consts[:, 0:1], 0.0)
            v.memset(consts[:, 1:2], SHIFT)
            v.memset(consts[:, 2:3], 1.0 / D)
            v.memset(consts[:, 3:4], EPS)
            v.memset(consts[:, 4:5], -0.5)
            v.memset(consts[:, 5:6], 1.0)
            v.memset(consts[:, 6:7], SCALE)
            return v.memset(consts[:, 7:8], -1.0)
        p.emit("vector", setup_consts, writes=["consts"])
        p.emit("vector", lambda v: v.memset(ones_c[:], 1.0), writes=["ones_c"])
        p.emit("vector", lambda v: v.memset(ones_r[:], 1.0), writes=["ones_r"])
        p.emit("sync", lambda e: e.dma_start(out=eye_s[:], in_=eye_d), writes=["eye"], dma=True, dsem="c0")
        p.emit("gpsimd", lambda e: e.dma_start(out=r2_s[:], in_=r2_d), writes=["r2"], dma=True, dsem="c1")
        p.emit("gpsimd", lambda e: e.dma_start(out=wr_s[:], in_=wr_d), writes=["wr"], dma=True, dsem="c2")
        p.emit("sync", lambda e: e.dma_start(out=bb_s[:], in_=bb_d), writes=["bb"], dma=True, dsem="c3")
        pass

        def wslice_dma(dst_ap, src_dram, col0, w):
            src = src_dram[:, col0:col0 + w].rearrange("(c p) o -> p c o", p=128)
            dst = dst_ap.rearrange("p (c o) -> p c o", c=nD)

            def fn(e, src=src, dst=dst):
                return e.dma_start(out=dst, in_=src)
            return fn

        ar.push()
        xstage = ar.alloc("xstage", 3 * S, F32)
        sq_s = ar.alloc("sqs", 2 * S, BF16)
        mb_s = ar.alloc("mbs", S, BF16)
        rb_s = ar.alloc("rbs", S, BF16)
        srow = ar.alloc("srowA", S, F32)
        srow2 = ar.alloc("srow2A", S, F32)
        srow3 = sq_s.bitcast(F32)[0:1, 0:S]
        brow = ar.alloc("browA", S, BF16)

        for c in range(nD):
            par = c % 3
            xs = xstage[:, par * S:(par + 1) * S]
            sq = sq_s[:, (c % 2) * S:(c % 2 + 1) * S]
            kx, ks = f"xst{par}", f"sq{c % 2}"

            def dldx(e, xs=xs, c=c):
                return e.dma_start(out=xs, in_=x_d[c * 128:(c + 1) * 128, :])
            if c % 2 == 0:
                p.emit("sync", dldx, writes=[kx], dma=True, dsem=f"x{par}")
            else:
                p.emit("gpsimd", dldx, writes=[kx], dma=True, dsem=f"gx{par}")

            def cast(v, c=c, xs=xs):
                return v.tensor_copy(xn(c), xs)
            p.emit("vector", cast, reads=[kx], writes=[f"xn{c}"])

            def square(s, sq=sq, xs=xs):
                return s.activation(sq, xs, AF.Square, bias=ck("zero"))
            p.emit("scalar", square, reads=[kx, "consts"], writes=[ks])

            def statmm(t, sq=sq, c=c):
                last = None
                for (o0, w) in _slices(half):
                    st, sp = (c == 0), (c == nD - 1)
                    last = t.matmul(PA[0:1, o0:o0 + w], lhsT=ones_c[:], rhs=xn(c)[:, o0:o0 + w], start=st, stop=sp)
                    last = t.matmul(PB[0:1, o0:o0 + w], lhsT=ones_c[:], rhs=xn(c)[:, half + o0:half + o0 + w], start=st, stop=sp)
                    last = t.matmul(PC[0:1, o0:o0 + w], lhsT=ones_c[:], rhs=sq[:, o0:o0 + w], start=st, stop=sp)
                    last = t.matmul(PD[0:1, o0:o0 + w], lhsT=ones_c[:], rhs=sq[:, half + o0:half + o0 + w], start=st, stop=sp)
                return last
            p.emit("tensor", statmm, reads=[f"xn{c}", ks, "ones_c"],
                   writes=["@PA", "@PB", "@PC", "@PD"])

        p.emit("gpsimd", wslice_dma(wk_s[:, 0:W], wk_d, 0, dh), writes=["wk0"], dma=True, dsem="wk0")
        p.emit("gpsimd", wslice_dma(wq_s[:, 0:W], wq_d, 0, dh), writes=["wq0"], dma=True, dsem="wq0")
        p.emit("gpsimd", wslice_dma(wv_s[:, 0:W], wv_d, 0, dh), writes=["wv0"], dma=True, dsem="wv0")

        p.emit("vector", lambda v: v.tensor_scalar(srow[0:1, 0:half], PA[0:1, 0:half], ck("invD", 1), None, ALU.mult),
               reads=["@PA", "consts"], writes=["srow"])
        p.emit("vector", lambda v: v.tensor_scalar(srow[0:1, half:S], PB[0:1, 0:half], ck("invD", 1), None, ALU.mult),
               reads=["@PB", "consts"], writes=["srow"])
        p.emit("vector", lambda v: v.tensor_copy(brow[0:1, 0:S], srow[0:1, :]),
               reads=["srow"], writes=["brow"])

        def bc_row(dst_s, key):
            def mm(t):
                last = None
                for (o0, w) in _slices(S):
                    tgt = (PA if o0 < 1024 else PB)
                    last = t.matmul(tgt[:, (o0 % 1024):(o0 % 1024) + w], lhsT=ones_r[:],
                                    rhs=brow[0:1, o0:o0 + w], start=True, stop=True)
                return last
            p.emit("tensor", mm, reads=["brow", "ones_r"], writes=["@PA", "@PB"])
            p.emit("vector", lambda v: v.tensor_copy(dst_s[:, 0:min(S, 1024)], PA[:, 0:min(S, 1024)]),
                   reads=["@PA"], writes=[key])
            if S > 1024:
                p.emit("vector", lambda v: v.tensor_copy(dst_s[:, 1024:S], PB[:, 0:S - 1024]),
                       reads=["@PB"], writes=[key])
        bc_row(mb_s, "mb")
        p.emit("gpsimd", lambda e: e.dma_start(out=bw_s[:], in_=bw_d),
               reads=["mb"], writes=["bw"], dma=True, dsem="c4")
        p.emit("vector", lambda v: v.tensor_scalar(srow2[0:1, 0:half], PC[0:1, 0:half], ck("invD", 1), None, ALU.mult),
               reads=["@PC", "consts"], writes=["srow2"])
        p.emit("vector", lambda v: v.tensor_scalar(srow2[0:1, half:S], PD[0:1, 0:half], ck("invD", 1), None, ALU.mult),
               reads=["@PD", "consts"], writes=["srow2"])
        p.emit("vector", lambda v: v.tensor_mul(srow3[0:1, :], srow[0:1, :], srow[0:1, :]),
               reads=["srow", "sq0", "sq1"], writes=["srow3", "sq0"])
        p.emit("vector", lambda v: v.tensor_sub(srow2[0:1, :], srow2[0:1, :], srow3[0:1, :]),
               reads=["srow2", "srow3"], writes=["srow2"])
        p.emit("scalar", lambda s: s.activation(srow2[0:1, :], srow2[0:1, :], AF.Ln, bias=ck("eps", 1)),
               reads=["srow2", "consts"], writes=["srow2"])
        p.emit("scalar", lambda s: s.activation(srow2[0:1, :], srow2[0:1, :], AF.Exp,
                                                bias=ck("zero", 1), scale=ck("neghalf", 1)),
               reads=["srow2", "consts"], writes=["srow2"])

        p.emit("vector", lambda v: v.tensor_copy(brow[0:1, 0:S], srow2[0:1, :]),
               reads=["srow2"], writes=["brow"])
        bc_row(rb_s, "rb")

        for c in range(nD):
            def sub_m(v, c=c):
                return v.tensor_sub(xn(c), xn(c), mb_s)
            p.emit("vector", sub_m, reads=["mb", f"xn{c}"], writes=[f"xn{c}"])

        raccs = []
        for c in range(nD):
            def mul_r(v, c=c):
                return v.tensor_mul(xn(c), xn(c), rb_s)
            p.emit("vector", mul_r, reads=["rb", f"xn{c}"], writes=[f"xn{c}"])

            NTC = SO // 128
            rpar = c % 2

            def routerc(t, c=c, rpar=rpar, NTC=NTC):
                last = None
                for tc in range(NTC):
                    last = t.matmul(PC[:, rpar * NTC * E + tc * E:rpar * NTC * E + (tc + 1) * E],
                                    lhsT=xn(c)[:, tc * 128:(tc + 1) * 128],
                                    rhs=wr_s[:, c * E:(c + 1) * E],
                                    start=True, stop=True)
                return last
            p.emit("tensor", routerc, reads=[f"xn{c}", "wr"], writes=["@PC"])

            def kmm0(t, c=c):
                last = None
                st, sp = (c == 0), (c == nD - 1)
                for (o0, w) in _slices(half):
                    last = t.matmul(PA[:, o0:o0 + w], lhsT=wk_s[:, c * dh:(c + 1) * dh],
                                    rhs=xn(c)[:, o0:o0 + w], start=st, stop=sp)
                    last = t.matmul(PB[:, o0:o0 + w], lhsT=wk_s[:, c * dh:(c + 1) * dh],
                                    rhs=xn(c)[:, half + o0:half + o0 + w], start=st, stop=sp)
                for (o0, w) in _slices(SO):
                    last = t.matmul(PD[:, o0:o0 + w], lhsT=wq_s[:, c * dh:(c + 1) * dh],
                                    rhs=xn(c)[:, o0:o0 + w], start=st, stop=sp)
                return last
            p.emit("tensor", kmm0, reads=["wk0", "wq0", f"xn{c}", "@PA", "@PB", "@PD"],
                   writes=["@PA", "@PB", "@PD"])

            def racc(v, c=c, rpar=rpar, NTC=NTC):
                pcs = PC[:, rpar * NTC * E:(rpar + 1) * NTC * E]
                if c == 0:
                    return v.tensor_copy(lg_s, pcs)
                return v.tensor_add(lg_s, lg_s, pcs)
            if c >= 1:
                p.emit("vector", raccs[c - 1], reads=["@PC", "lg"], writes=["lg"])
            raccs.append(racc)
        p.emit("vector", raccs[nD - 1], reads=["@PC", "lg"], writes=["lg"])
        p.emit("vector", lambda v: v.engine_nop(),
               reads=[f"xn{c}" for c in range(nD)], writes=["xn"])

        ar.pop()

        def emit_mask_dve():
            for tc in range(SO // 128):
                def rmax(v, tc=tc):
                        return v.reduce_max(mx_s[:, tc:tc + 1], lg_s[:, tc * E:(tc + 1) * E], axis=AX.X)
                p.emit("vector", rmax, reads=["lg"], writes=["mx"])

                def mkmask(v, tc=tc):
                        return v.tensor_scalar(mtm_s[:, tc * E:(tc + 1) * E], lg_s[:, tc * E:(tc + 1) * E],
                                                       mx_s[:, tc:tc + 1], None, ALU.is_equal)
                p.emit("vector", mkmask, reads=["lg", "mx"], writes=["masktm"])

        def emit_mask_pe():
            def masktr(t):
                last = None
                for tc in range(SO // 128):
                        last = t.transpose(PAb[0:E, tc * 128:(tc + 1) * 128],
                                                   mtm_s[:, tc * E:(tc + 1) * E], eye_s[:])
                return last
            p.emit("tensor", masktr, reads=["masktm", "eye"], writes=["@PA"])
            p.emit("vector", lambda v: v.tensor_copy(mfm_s[0:E, 0:SO], PAb[0:E, 0:SO]),
                   reads=["@PA"], writes=["maskfm"])

            def maskrep(t):
                last = None
                for (o0, w) in _slices(SO):
                        last = t.matmul(PB[0:ER, o0:o0 + w], lhsT=r2_s[:],
                                                rhs=mfm_s[0:E, o0:o0 + w], start=True, stop=True)
                return last
            p.emit("tensor", maskrep, reads=["maskfm", "r2", "@PB"], writes=["@PB"])
            p.emit("vector", lambda v: v.tensor_copy(mask16[0:ER, 0:SO], PB[0:ER, 0:SO]),
                   reads=["@PB"], writes=["mask16"])

        ar.push()
        k_sb = ar.alloc("k_b", S, BF16)
        vf_sb = ar.alloc("vf_b", S, BF16)
        vt_sb = vf_sb
        q_sb = ar.alloc("q_b", SO, BF16)
        p_sb = ar.alloc("p_b", 2 * SO, BF16)
        acc_s = ar.alloc("acc_b", nD * SO, BF16)
        wb_s = ar.alloc("wb_b", 2 * W, BF16)
        hm_s = ar.alloc("hm_b", SO, BF16)
        x2_s = ar.alloc("x2_b", 2 * SO, F32)
        PDb = PD[:].bitcast(BF16)

        def acc(j):
            return acc_s[:, j * SO:(j + 1) * SO]

        PSBT = (PA, PB)
        PSKEY = ("@PA", "@PB")

        def kvmm(t, ws, par, lo, hi):
            last = None
            for (o0, w) in _slices(S, min(NS, half)):
                tgt = (lo if o0 < half else hi)
                for c in range(nD):
                    last = t.matmul(tgt[:, (o0 % half):(o0 % half) + w],
                                    lhsT=ws[:, par * W + c * dh:par * W + (c + 1) * dh],
                                    rhs=xn(c)[:, o0:o0 + w],
                                    start=(c == 0), stop=(c == nD - 1))
            return last

        def finalize_a(u):
            sp = u % 2

            def pssum(t, sp=sp):
                last = None
                for (o0, w) in _slices(SO):
                    last = t.matmul(PD[0:1, o0:o0 + w], lhsT=ones_c[:],
                                    rhs=sum_bf[:, sp * SO + o0:sp * SO + o0 + w],
                                    start=True, stop=True)
                return last
            p.emit("vector", (lambda v, u=u: v.tensor_copy(attn_s[:, u * SO:(u + 1) * SO], PC[:, 0:SO])),
                   reads=["@PC"], writes=["attn"])
            p.emit("tensor", pssum, reads=[f"sumacc{sp}", "ones_c", "@PD"], writes=["@PD"])
            p.emit("scalar", lambda s: s.activation(srowb[0:1, 0:SO], PD[0:1, 0:SO], AF.Ln, bias=ck("zero", 1)),
                   reads=["@PD", "consts"], writes=["srowB"])
            p.emit("scalar", lambda s: s.activation(srowb[0:1, 0:SO], srowb[0:1, 0:SO], AF.Exp,
                                                    bias=ck("zero", 1), scale=ck("mone", 1)),
                   reads=["srowB", "consts"], writes=["srowB"])
            p.emit("vector", lambda v: v.tensor_copy(browb[0:1, 0:SO], srowb[0:1, 0:SO]),
                   reads=["srowB"], writes=["browB"])

        def finalize_b(u):
            def bcrec(t):
                last = None
                for (o0, w) in _slices(SO):
                    last = t.matmul(PB[:, o0:o0 + w], lhsT=ones_r[:],
                                    rhs=browb[0:1, o0:o0 + w], start=True, stop=True)
                return last
            p.emit("tensor", bcrec, reads=["browB", "ones_r", "@PB"], writes=["@PB"])
            p.emit("vector", lambda v: v.tensor_copy(recipb, PB[:, 0:SO]),
                   reads=["@PB"], writes=["recipb"])

            def norm(v, u=u):
                a = attn_s[:, u * SO:(u + 1) * SO]
                return v.tensor_mul(a, a, recipb)
            p.emit("vector", norm, reads=["recipb", "attn"], writes=["attn"])

        def scores_f(t, tkc):
            tgt = PSBT[tkc % 2]
            last = None
            for (o0, w) in _slices(SO):
                last = t.matmul(tgt[:, o0:o0 + w],
                                lhsT=k_sb[:, tkc * 128:(tkc + 1) * 128],
                                rhs=q_sb[:, o0:o0 + w], start=True, stop=True)
            return last

        def expev_f(s, tkc):
            src = PSBT[tkc % 2]
            return s.activation(p_sb[:, (tkc % 2) * SO:(tkc % 2 + 1) * SO],
                                src[:, 0:SO], AF.Exp, bias=ck("shift"), scale=ck("scale"))

        def collapse_f(v, tkc, u):
            dst = sum_bf[:, (u % 2) * SO:(u % 2 + 1) * SO]
            pc_ = p_sb[:, (tkc % 2) * SO:(tkc % 2 + 1) * SO]
            if tkc == 0:
                return v.tensor_copy(dst, pc_)
            return v.tensor_add(dst, dst, pc_)

        def av_f(t, tkc):
            last = None
            for (o0, w) in _slices(SO):
                last = t.matmul(PC[:, o0:o0 + w],
                                lhsT=vt_sb[:, tkc * 128:(tkc + 1) * 128],
                                rhs=p_sb[:, (tkc % 2) * SO + o0:(tkc % 2) * SO + o0 + w],
                                start=(tkc == 0), stop=(tkc == nT - 1))
            return last

        for h in range(H):
            par = h % 2
            kwq, kwk, kwv = f"wq{par}", f"wk{par}", f"wv{par}"
            thr = ["mb"] if h <= 2 else []
            if h > 0:
                p.emit("gpsimd", wslice_dma(wq_s[:, par * W:(par + 1) * W], wq_d, h * dh, dh),
                       reads=thr, writes=[kwq], dma=True, dsem=f"wq{par}")
                p.emit("gpsimd", wslice_dma(wk_s[:, par * W:(par + 1) * W], wk_d, h * dh, dh),
                       reads=thr, writes=[kwk], dma=True, dsem=f"wk{par}")
                p.emit("gpsimd", wslice_dma(wv_s[:, par * W:(par + 1) * W], wv_d, h * dh, dh),
                       reads=thr, writes=[kwv], dma=True, dsem=f"wv{par}")
            kwb = f"wb{par}"
            p.emit("gpsimd", wslice_dma(wb_s[:, par * W:(par + 1) * W], wb_d, h * 128, 128),
                   reads=thr, writes=[kwb], dma=True, dsem=f"wb{par}")

            if h > 0:
                p.emit("tensor", (lambda t, par=par, f=kvmm: f(t, wk_s, par, PA, PB)),
                       reads=[kwk, "xn"], writes=["@PA", "@PB"])
            p.emit("scalar", lambda s: s.copy(k_sb[:, 0:half], PA[:, 0:half]),
                   reads=["@PA"], writes=["ksb"])
            p.emit("scalar", lambda s: s.copy(k_sb[:, half:S], PB[:, 0:half]),
                   reads=["@PB"], writes=["ksb"])
            if h == 0:
                p.emit("scalar", lambda s: s.copy(q_sb, PD[:, 0:SO]),
                       reads=["@PD"], writes=["qsb"])
                emit_mask_dve()
            if h > 0:
                finalize_a(h - 1)

            if h > 0:
                def qmm(t, par=par):
                    last = None
                    for (o0, w) in _slices(SO):
                        for c in range(nD):
                            last = t.matmul(PA[:, o0:o0 + w],
                                            lhsT=wq_s[:, par * W + c * dh:par * W + (c + 1) * dh],
                                            rhs=xn(c)[:, o0:o0 + w],
                                            start=(c == 0), stop=(c == nD - 1))
                    return last
                p.emit("tensor", qmm, reads=[kwq, "xn", "@PA"], writes=["@PA"])
                p.emit("scalar", lambda s: s.copy(q_sb, PA[:, 0:SO]),
                       reads=["@PA"], writes=["qsb"])

            def vlo(t, par=par):
                last = None
                for c in range(nD):
                    for (o0, w) in _slices(half):
                        last = t.matmul(PC[:, o0:o0 + w],
                                        lhsT=wv_s[:, par * W + c * dh:par * W + (c + 1) * dh],
                                        rhs=xn(c)[:, o0:o0 + w],
                                        start=(c == 0), stop=(c == nD - 1))
                return last
            p.emit("tensor", vlo, reads=[kwv, "xn", "@PC"], writes=["@PC"])
            p.emit("scalar", lambda s: s.copy(vf_sb[:, 0:half], PC[:, 0:half]),
                   reads=["@PC"], writes=["vflo"])

            def vhi(t, par=par):
                last = None
                for c in range(nD):
                    for (o0, w) in _slices(half):
                        last = t.matmul(PD[:, o0:o0 + w],
                                        lhsT=wv_s[:, par * W + c * dh:par * W + (c + 1) * dh],
                                        rhs=xn(c)[:, half + o0:half + o0 + w],
                                        start=(c == 0), stop=(c == nD - 1))
                return last
            p.emit("tensor", vhi, reads=[kwv, "xn", "@PD"], writes=["@PD"])
            p.emit("scalar", lambda s: s.copy(vf_sb[:, half:S], PD[:, 0:half]),
                   reads=["@PD"], writes=["vfhi"])
            if h == 0:
                emit_mask_pe()

            def vtlo(t):
                last = None
                for c in range(nT // 2):
                    last = t.transpose(PCb[:, c * 128:(c + 1) * 128],
                                       vf_sb[:, c * 128:(c + 1) * 128], eye_s[:])
                return last
            p.emit("tensor", vtlo, reads=["vflo", "eye", "@PC"], writes=["@PC"])
            p.emit("scalar", lambda s: s.copy(vt_sb[:, 0:half], PCb[:, 0:half]),
                   reads=["@PC"], writes=["vt", "vflo"])
            if h > 0:
                finalize_b(h - 1)

            def vthi(t):
                last = None
                for c in range(nT // 2, nT):
                    last = t.transpose(PDb[:, c * 128:(c + 1) * 128],
                                       vf_sb[:, c * 128:(c + 1) * 128], eye_s[:])
                return last
            p.emit("tensor", vthi, reads=["vfhi", "eye", "@PD"], writes=["@PD"])

            NSL = len(_slices(SO))
            BITEMS = [(sl, c) for sl in range(NSL) for c in range(nD)]
            BSH = min(2, nT - 1)
            M0 = (NSL * nD + (nT - BSH) - 1) // (nT - BSH)

            def basemm(t, tkc, par=par, h=h, BITEMS=BITEMS, M0=M0):
                last = None
                for (sl, c) in BITEMS[(tkc - BSH) * M0:(tkc - BSH + 1) * M0]:
                    (o0, w) = _slices(SO)[sl]
                    last = t.matmul(PD[:, o0:o0 + w],
                                    lhsT=wb_s[:, par * W + c * dh:par * W + (c + 1) * dh],
                                    rhs=xn(c)[:, o0:o0 + w],
                                    start=(c == 0), stop=(c == nD - 1))
                return last

            for tkc in range(nT):
                p.emit("tensor", (lambda t, tkc=tkc, f=scores_f: f(t, tkc)),
                       reads=["ksb", "qsb", PSKEY[tkc % 2]], writes=[PSKEY[tkc % 2]])
                p.emit("scalar", (lambda s, tkc=tkc, f=expev_f: f(s, tkc)),
                       reads=[PSKEY[tkc % 2], "consts"], writes=[f"p{tkc % 2}"])
                p.emit("vector", (lambda v, tkc=tkc, h=h, f=collapse_f: f(v, tkc, h)),
                       reads=[f"p{tkc % 2}", f"sumacc{h % 2}"], writes=[f"sumacc{h % 2}"])
                if tkc >= 1:
                    p.emit("tensor", (lambda t, tkc=tkc, f=av_f: f(t, tkc - 1)),
                           reads=[f"p{(tkc - 1) % 2}", "vt"], writes=["@PC"])
                if tkc == 1:
                    p.emit("scalar", lambda s: s.copy(vt_sb[:, half:S], PDb[:, half:S]),
                           reads=["@PD"], writes=["vt", "vfhi"])
                if tkc >= BSH and BITEMS[(tkc - BSH) * M0:(tkc - BSH + 1) * M0]:
                    p.emit("tensor", (lambda t, tkc=tkc, f=basemm: f(t, tkc)),
                           reads=[kwb, "xn", "@PD"], writes=["@PD"])
            p.emit("tensor", (lambda t, f=av_f: f(t, nT - 1)),
                   reads=[f"p{(nT - 1) % 2}", "vt"], writes=["@PC"])

            def baev(v, h=h):
                return v.tensor_scalar(acc(h), PD[:, 0:SO], bb_s[:, h:h + 1], None, ALU.add)
            p.emit("vector", baev, reads=["@PD", "bb"], writes=[f"acc{h}"])

        at_s = wq_s[:, 0:W]

        p.emit("gpsimd", wslice_dma(at_s, at_d, 0, ER),
               writes=["at", "wq0"], dma=True, dsem="at", fresh_ok=True)

        def hmm_(t):
            last = None
            for (o0, w) in _slices(SO):
                for c in range(nD):
                    last = t.matmul(PA[0:ER, o0:o0 + w],
                                    lhsT=at_s[:, c * ER:(c + 1) * ER],
                                    rhs=xn(c)[:, o0:o0 + w],
                                    start=(c == 0), stop=(c == nD - 1))
            return last
        finalize_a(H - 1)
        p.emit("tensor", hmm_, reads=["at", "xn", "@PA"], writes=["@PA"])
        finalize_b(H - 1)
        p.emit("vector", lambda v: v.tensor_mul(hm_s[0:ER, 0:SO], PA[0:ER, 0:SO], mask16[0:ER, 0:SO]),
               reads=["@PA", "mask16"], writes=["hm"])

        last_store = None
        WOBUF = (wk_s[:, 0:W], wv_s[:, 0:W], wk_s[:, W:2 * W], wv_s[:, W:2 * W])
        WOPRV = ("wk0", "wv0", "wk1", "wv1")
        for j in range(nD):
            par = j % 2
            p4 = j % 4
            kw = f"wo{p4}"
            p.emit("gpsimd", wslice_dma(WOBUF[p4], wo_d, j * 128, 128),
                   writes=[kw, WOPRV[p4]], dma=True, dsem=f"wx{p4}", fresh_ok=True)
            OPP = (PC, PD)[j % 2]
            OPK = ("@PC", "@PD")[j % 2]

            def opmm(t, p4=p4, OPP=OPP):
                last = None
                wb_ = WOBUF[p4]
                for (o0, w) in _slices(SO):
                    for c in range(H):
                        last = t.matmul(OPP[:, o0:o0 + w],
                                        lhsT=wb_[:, c * 128:(c + 1) * 128],
                                        rhs=attn_s[:, c * SO + o0:c * SO + o0 + w],
                                        start=(c == 0), stop=(c == H - 1))
                return last
            p.emit("tensor", opmm, reads=[kw, "attn", OPK], writes=[OPK])

            def opev(v, j=j, OPP=OPP):
                return v.scalar_tensor_tensor(acc(j), OPP[:, 0:SO], ck("one"),
                                              acc(j), ALU.mult, ALU.add)
            p.emit("vector", opev, reads=[OPK, "consts", f"acc{j}"], writes=[f"acc{j}"])

            DPP = (PA, PB)[j % 2]
            DPK = ("@PA", "@PB")[j % 2]

            def demm(t, j=j, DPP=DPP):
                last = None
                for (o0, w) in _slices(SO):
                    last = t.matmul(DPP[:, o0:o0 + w],
                                    lhsT=bw_s[:, j * 128:(j + 1) * 128],
                                    rhs=hm_s[0:ER, o0:o0 + w], start=True, stop=True)
                return last
            p.emit("tensor", demm, reads=["bw", "hm", DPK], writes=[DPK])

            def deev(v, j=j, DPP=DPP):
                return v.scalar_tensor_tensor(acc(j), DPP[:, 0:SO], ck("one"),
                                              acc(j), ALU.mult, ALU.add)
            p.emit("vector", deev, reads=[DPK, "consts", f"acc{j}"], writes=[f"acc{j}"])

            kx = f"x2{par}"

            def dldx3(e, j=j, par=par):
                return e.dma_start(out=x2_s[:, par * SO:(par + 1) * SO],
                                   in_=x_d[j * 128:(j + 1) * 128, 0:SO])
            p.emit("sync", dldx3, writes=[kx], dma=True, dsem=f"x{par}")

            def resadd(v, j=j, par=par):
                xs = x2_s[:, par * SO:(par + 1) * SO]
                return v.tensor_add(xs, xs, acc(j))
            p.emit("vector", resadd, reads=[kx, f"acc{j}"], writes=[kx])

            def store(e, j=j, par=par):
                return e.dma_start(out=y_d[j * 128:(j + 1) * 128, :],
                                   in_=x2_s[:, par * SO:(par + 1) * SO])
            last_store = p.emit("scalar", store, reads=[kx], writes=["y"], dma=True, dsem="out")
        ar.pop()

        p.wait("scalar", last_store)
        p.replay(nc, top)
    return nc


def _prep_shared(Wq, Wk, Wv, Wo, W_base, b_base, router_W, A, Bw, D, E, R):
    nD = D // 128
    ER = E * R
    return {
        "wq": np.ascontiguousarray(np.asarray(Wq, np.float32).T),
        "wk": np.ascontiguousarray(np.asarray(Wk, np.float32).T),
        "wv": np.ascontiguousarray(np.asarray(Wv, np.float32).T),
        "wo": np.ascontiguousarray(np.asarray(Wo, np.float32).T),
        "wb": np.ascontiguousarray(np.asarray(W_base, np.float32).T),
        "wr": np.ascontiguousarray(
            np.asarray(router_W, np.float32).T.reshape(nD, 128, E)
            .transpose(1, 0, 2).reshape(128, nD * E)),
        "at": np.ascontiguousarray(np.asarray(A, np.float32).reshape(ER, D).T),
        "bw": np.ascontiguousarray(
            np.asarray(Bw, np.float32).transpose(0, 2, 1).reshape(ER, D)),
        "bb": np.ascontiguousarray(np.asarray(b_base, np.float32).reshape(nD, 128).T),
        "r2": np.ascontiguousarray(np.kron(np.eye(E, dtype=np.float32),
                                           np.ones((1, R), np.float32)) * 2.0),
        "eye": np.eye(128, dtype=ml_dtypes.bfloat16),
    }


def kernel(L0_attn_hidden_states, L0_attn_Wq, L0_attn_Wk, L0_attn_Wv,
           L0_attn_Wo, L0_ffn_W_base, L0_ffn_b_base, L0_ffn_router_W,
           L0_ffn_A, L0_ffn_Bw):
    D, S, SO, H, E, R = 2048, 2048, 1024, 16, 8, 16
    x = np.asarray(L0_attn_hidden_states, dtype=np.float32)
    B = x.shape[0]
    x_fm = np.ascontiguousarray(x.reshape(B * S, D).T)
    shared = _prep_shared(L0_attn_Wq, L0_attn_Wk, L0_attn_Wv, L0_attn_Wo,
                          L0_ffn_W_base, L0_ffn_b_base, L0_ffn_router_W,
                          L0_ffn_A, L0_ffn_Bw, D, E, R)
    in_maps = []
    for core in range(2 * B):
        b, g = divmod(core, 2)
        seq = x_fm[:, b * S:(b + 1) * S]
        own = seq[:, g * SO:(g + 1) * SO]
        part = np.concatenate([seq[:, :g * SO], seq[:, (g + 1) * SO:]], axis=1)
        m = dict(shared)
        m["x"] = np.ascontiguousarray(np.concatenate([own, part], axis=1))
        in_maps.append(m)

    nc = build(D=D, S=S, SO=SO, H=H, E=E, R=R)
    res = run_bass_kernel_spmd(nc, in_maps, core_ids=list(range(2 * B)))
    out_fm = np.empty((D, B * S), dtype=np.float32)
    for core in range(2 * B):
        b, g = divmod(core, 2)
        out_fm[:, b * S + g * SO:b * S + (g + 1) * SO] = res.results[core]["y"]
    return np.ascontiguousarray(out_fm.T).reshape(B, S, D)


# version 44 (final)
# speedup vs baseline: 1.0048x; 1.0048x over previous
import numpy as np
import ml_dtypes
import concourse.bass as bass
import concourse.mybir as mybir
from concourse.bass_utils import run_bass_kernel_spmd

F32 = mybir.dt.float32
BF16 = mybir.dt.bfloat16
AF = mybir.ActivationFunctionType
ALU = mybir.AluOpType
AX = mybir.AxisListType

NS = 512


def _slices(total, width=NS):
    return [(i * width, min(width, total - i * width))
            for i in range((total + width - 1) // width)]


class Plan:
    ENGINES = ("sync", "scalar", "vector", "tensor", "gpsimd")
    ENG_SEM = {"sync": "e_sync", "scalar": "e_act", "vector": "e_dve",
               "tensor": "e_pe", "gpsimd": "e_pool"}

    def __init__(self):
        self.ops = {e: [] for e in self.ENGINES}
        self.sem_count = {}
        self.res = {}
        self.touched = set()
        self.hazard = []
        self.waited = {e: {} for e in self.ENGINES}

    def emit(self, eng, fn, reads=(), writes=(), dma=False, dsem=None, fresh_ok=False):
        need = {}

        def add(tok):
            sem, val = tok
            if need.get(sem, 0) < val:
                need[sem] = val

        for r in reads:
            e = self.res.get(r)
            if e is None:
                continue
            if e["w"] is not None:
                add(e["w"][1])
            if r.startswith("@"):
                for (re_, rt) in e["r"]:
                    if re_ != eng:
                        add(rt)
        for w in writes:
            e = self.res.get(w)
            if e is None:
                if not fresh_ok:
                    for t in self.hazard:
                        add(t)
                continue
            if e["w"] is not None:
                add(e["w"][1])
            for (_, rt) in e["r"]:
                add(rt)
        for sem in sorted(need):
            val = need[sem]
            if self.waited[eng].get(sem, 0) < val:
                self.ops[eng].append(("wait", sem, val))
                self.waited[eng][sem] = val
        if dma:
            assert dsem is not None, "dma emits need a per-stream dsem"
            sem, inc = "d_" + dsem, 16
        else:
            sem, inc = self.ENG_SEM[eng], 1
        val = self.sem_count.get(sem, 0) + inc
        self.sem_count[sem] = val
        tok = (sem, val)
        self.ops[eng].append(("call", fn, sem, inc))
        for r in reads:
            self.res.setdefault(r, {"w": None, "r": []})["r"].append((eng, tok))
        for w in writes:
            self.res[w] = {"w": (eng, tok), "r": []}
        self.touched.update(reads)
        self.touched.update(writes)
        return tok

    def settle(self, keys, tok):
        for k in keys:
            e = self.res.setdefault(k, {"w": None, "r": []})
            e["w"] = ("dma", tok)

    def wait(self, eng, tok):
        sem, val = tok
        if self.waited[eng].get(sem, 0) < val:
            self.ops[eng].append(("wait", sem, val))
            self.waited[eng][sem] = val

    def replay(self, nc, stack):
        names = set(self.ENG_SEM.values())
        for ops in self.ops.values():
            for op in ops:
                names.add(op[1] if op[0] == "wait" else op[2])
        sems = {}
        for name in sorted(names):
            sems[name] = stack.enter_context(nc.semaphore(name))
        block = stack.enter_context(nc.Block())

        def mk(engname):
            def body(engine):
                for op in self.ops[engname]:
                    if op[0] == "wait":
                        engine.wait_ge(sems[op[1]], op[2])
                    else:
                        _, fn, sem, inc = op
                        fn(engine).then_inc(sems[sem], inc)
            return body

        for engname in self.ENGINES:
            if self.ops[engname]:
                getattr(block, engname)(mk(engname))


class Arena:

    def __init__(self, nc, stack, cols_f32, plan):
        self.t = stack.enter_context(nc.sbuf_tensor("arena", [128, cols_f32], F32))
        self.tb = self.t[:].bitcast(BF16)
        self.cols = cols_f32
        self.plan = plan
        self.off = 0
        self.peak = 0
        self.marks = []
        self.live = []
        self.hazards = []

    def alloc(self, key, cols, dtype):
        f32cols = cols if dtype == F32 else (cols + 1) // 2
        start = self.off
        self.off += f32cols
        self.peak = max(self.peak, self.off)
        assert self.off <= self.cols, f"arena overflow at {key}: {self.off} > {self.cols}"
        if key is not None:
            self.live.append(key)
        if dtype == F32:
            return self.t[:, start:start + cols]
        return self.tb[:, 2 * start:2 * start + cols]

    def alloc_top(self, key, cols, dtype):
        f32cols = cols if dtype == F32 else (cols + 1) // 2
        self.cols -= f32cols
        start = self.cols
        assert self.off <= self.cols, f"arena top overflow at {key}"
        if dtype == F32:
            return self.t[:, start:start + cols]
        return self.tb[:, 2 * start:2 * start + cols]

    def push(self):
        self.marks.append((self.off, self.live, set(self.plan.touched)))
        self.live = []

    def pop(self):
        off, live, snap = self.marks.pop()
        toks = []
        for k in self.plan.touched - snap:
            e = self.plan.res.get(k)
            if e:
                if e["w"] is not None:
                    toks.append(e["w"][1])
                toks += [t for (_, t) in e["r"]]
        best = {}
        for (s, v) in toks + self.hazards:
            if best.get(s, 0) < v:
                best[s] = v
        self.hazards = sorted(best.items())
        self.plan.hazard = list(self.hazards)
        self.off, self.live = off, live


def build(D=2048, S=2048, SO=1024, H=16, E=8, R=16):
    dh = 128
    assert H * dh == D and D % 128 == 0 and S % 128 == 0 and SO % 128 == 0
    nD = D // 128
    nT = S // 128
    ER = E * R
    SCALE = float(dh) ** -0.5
    SHIFT = -24.0
    EPS = 1e-5
    half = S // 2
    assert half <= 1024 and SO <= 1024

    nc = bass.Bass("TRN2", target_bir_lowering=False, debug=False)
    x_d = nc.dram_tensor("x", [D, S], F32, kind="ExternalInput").ap()
    wq_d = nc.dram_tensor("wq", [D, D], F32, kind="ExternalInput").ap()
    wk_d = nc.dram_tensor("wk", [D, D], F32, kind="ExternalInput").ap()
    wv_d = nc.dram_tensor("wv", [D, D], F32, kind="ExternalInput").ap()
    wo_d = nc.dram_tensor("wo", [D, D], F32, kind="ExternalInput").ap()
    wb_d = nc.dram_tensor("wb", [D, D], F32, kind="ExternalInput").ap()
    wr_d = nc.dram_tensor("wr", [128, nD * E], F32, kind="ExternalInput").ap()
    at_d = nc.dram_tensor("at", [D, ER], F32, kind="ExternalInput").ap()
    bw_d = nc.dram_tensor("bw", [ER, D], F32, kind="ExternalInput").ap()
    bb_d = nc.dram_tensor("bb", [128, nD], F32, kind="ExternalInput").ap()
    r2_d = nc.dram_tensor("r2", [E, ER], F32, kind="ExternalInput").ap()
    eye_d = nc.dram_tensor("eye", [128, 128], BF16, kind="ExternalInput").ap()
    y_d = nc.dram_tensor("y", [D, SO], F32, kind="ExternalOutput").ap()

    p = Plan()
    from contextlib import ExitStack
    with ExitStack() as top:
        EC = top.enter_context
        consts = EC(nc.sbuf_tensor("consts", [128, 8], F32))
        ones_c = EC(nc.sbuf_tensor("ones_c", [128, 1], BF16))
        ones_r = EC(nc.sbuf_tensor("ones_r", [1, 128], BF16))
        eye_s = EC(nc.sbuf_tensor("eye_s", [128, 128], BF16))
        r2_s = EC(nc.sbuf_tensor("r2_s", [E, ER], BF16))
        wr_s = EC(nc.sbuf_tensor("wr_s", [128, nD * E], BF16))
        bb_s = EC(nc.sbuf_tensor("bb_s", [128, nD], F32))
        bw_s = EC(nc.sbuf_tensor("bw_s", [ER, D], BF16))
        xn_s = EC(nc.sbuf_tensor("xn_s", [128, nD * S], BF16))
        attn_s = EC(nc.sbuf_tensor("attn_s", [128, H * SO], BF16))
        mask16 = EC(nc.sbuf_tensor("mask16", [128, SO], BF16))
        PA = EC(nc.psum_tensor("PA", [128, 1024], F32))
        PB = EC(nc.psum_tensor("PB", [128, 1024], F32))
        PC = EC(nc.psum_tensor("PC", [128, 1024], F32))
        PD = EC(nc.psum_tensor("PD", [128, 1024], F32))
        PCb = PC[:].bitcast(BF16)
        PAb = PA[:].bitcast(BF16)

        ar = Arena(nc, top, 26400, p)
        W = nD * dh
        wq_s = ar.alloc_top("wq_b", 2 * W, BF16)
        wk_s = ar.alloc_top("wk_b", 2 * W, BF16)
        wv_s = ar.alloc_top("wv_b", 2 * W, BF16)
        sum_bf = ar.alloc("sum_b", 2 * SO, BF16)
        srowb = ar.alloc("srowB", SO, F32)
        browb = ar.alloc("browB", SO, BF16)
        recipb = ar.alloc("rcb_b", SO, BF16)
        lg_s = ar.alloc("lgs", (SO // 128) * E, F32)
        mx_s = ar.alloc("mxs", SO // 128, F32)
        mtm_s = ar.alloc("mtms", (SO // 128) * E, BF16)
        mfm_s = ar.alloc("mfms", SO, BF16)

        CK = {"zero": 0, "shift": 1, "invD": 2, "eps": 3, "neghalf": 4,
              "one": 5, "scale": 6, "mone": 7}

        def ck(name, rows=128):
            i = CK[name]
            return consts[0:rows, i:i + 1]

        def xn(c):
            return xn_s[:, c * S:(c + 1) * S]

        def setup_consts(v):
            v.memset(consts[:, 0:1], 0.0)
            v.memset(consts[:, 1:2], SHIFT)
            v.memset(consts[:, 2:3], 1.0 / D)
            v.memset(consts[:, 3:4], EPS)
            v.memset(consts[:, 4:5], -0.5)
            v.memset(consts[:, 5:6], 1.0)
            v.memset(consts[:, 6:7], SCALE)
            return v.memset(consts[:, 7:8], -1.0)
        p.emit("vector", setup_consts, writes=["consts"])
        p.emit("vector", lambda v: v.memset(ones_c[:], 1.0), writes=["ones_c"])
        p.emit("vector", lambda v: v.memset(ones_r[:], 1.0), writes=["ones_r"])
        p.emit("sync", lambda e: e.dma_start(out=eye_s[:], in_=eye_d), writes=["eye"], dma=True, dsem="c0")
        p.emit("gpsimd", lambda e: e.dma_start(out=r2_s[:], in_=r2_d), writes=["r2"], dma=True, dsem="c1")
        p.emit("gpsimd", lambda e: e.dma_start(out=wr_s[:], in_=wr_d), writes=["wr"], dma=True, dsem="c2")
        p.emit("sync", lambda e: e.dma_start(out=bb_s[:], in_=bb_d), writes=["bb"], dma=True, dsem="c3")
        pass

        def wslice_dma(dst_ap, src_dram, col0, w):
            src = src_dram[:, col0:col0 + w].rearrange("(c p) o -> p c o", p=128)
            dst = dst_ap.rearrange("p (c o) -> p c o", c=nD)

            def fn(e, src=src, dst=dst):
                return e.dma_start(out=dst, in_=src)
            return fn

        ar.push()
        xstage = ar.alloc("xstage", 3 * S, F32)
        sq_s = ar.alloc("sqs", 2 * S, BF16)
        mb_s = ar.alloc("mbs", S, BF16)
        rb_s = ar.alloc("rbs", S, BF16)
        srow = ar.alloc("srowA", S, F32)
        srow2 = ar.alloc("srow2A", S, F32)
        srow3 = sq_s.bitcast(F32)[0:1, 0:S]
        brow = ar.alloc("browA", S, BF16)

        for c in range(nD):
            par = c % 3
            xs = xstage[:, par * S:(par + 1) * S]
            sq = sq_s[:, (c % 2) * S:(c % 2 + 1) * S]
            kx, ks = f"xst{par}", f"sq{c % 2}"

            def dldx(e, xs=xs, c=c):
                return e.dma_start(out=xs, in_=x_d[c * 128:(c + 1) * 128, :])
            if c % 2 == 0:
                p.emit("sync", dldx, writes=[kx], dma=True, dsem=f"x{par}")
            else:
                p.emit("gpsimd", dldx, writes=[kx], dma=True, dsem=f"gx{par}")

            def cast(v, c=c, xs=xs):
                return v.tensor_copy(xn(c), xs)
            p.emit("vector", cast, reads=[kx], writes=[f"xn{c}"])

            def square(s, sq=sq, xs=xs):
                return s.activation(sq, xs, AF.Square, bias=ck("zero"))
            p.emit("scalar", square, reads=[kx, "consts"], writes=[ks])

            def statmm(t, sq=sq, c=c):
                last = None
                for (o0, w) in _slices(half):
                    st, sp = (c == 0), (c == nD - 1)
                    last = t.matmul(PA[0:1, o0:o0 + w], lhsT=ones_c[:], rhs=xn(c)[:, o0:o0 + w], start=st, stop=sp)
                    last = t.matmul(PB[0:1, o0:o0 + w], lhsT=ones_c[:], rhs=xn(c)[:, half + o0:half + o0 + w], start=st, stop=sp)
                    last = t.matmul(PC[0:1, o0:o0 + w], lhsT=ones_c[:], rhs=sq[:, o0:o0 + w], start=st, stop=sp)
                    last = t.matmul(PD[0:1, o0:o0 + w], lhsT=ones_c[:], rhs=sq[:, half + o0:half + o0 + w], start=st, stop=sp)
                return last
            p.emit("tensor", statmm, reads=[f"xn{c}", ks, "ones_c"],
                   writes=["@PA", "@PB", "@PC", "@PD"])

        p.emit("gpsimd", wslice_dma(wk_s[:, 0:W], wk_d, 0, dh), writes=["wk0"], dma=True, dsem="wk0")
        p.emit("gpsimd", wslice_dma(wq_s[:, 0:W], wq_d, 0, dh), writes=["wq0"], dma=True, dsem="wq0")
        p.emit("gpsimd", wslice_dma(wv_s[:, 0:W], wv_d, 0, dh), writes=["wv0"], dma=True, dsem="wv0")

        p.emit("vector", lambda v: v.tensor_scalar(srow[0:1, 0:half], PA[0:1, 0:half], ck("invD", 1), None, ALU.mult),
               reads=["@PA", "consts"], writes=["srow"])
        p.emit("vector", lambda v: v.tensor_scalar(srow[0:1, half:S], PB[0:1, 0:half], ck("invD", 1), None, ALU.mult),
               reads=["@PB", "consts"], writes=["srow"])
        p.emit("vector", lambda v: v.tensor_copy(brow[0:1, 0:S], srow[0:1, :]),
               reads=["srow"], writes=["brow"])

        def bc_row(dst_s, key):
            def mm(t):
                last = None
                for (o0, w) in _slices(S):
                    tgt = (PA if o0 < 1024 else PB)
                    last = t.matmul(tgt[:, (o0 % 1024):(o0 % 1024) + w], lhsT=ones_r[:],
                                    rhs=brow[0:1, o0:o0 + w], start=True, stop=True)
                return last
            p.emit("tensor", mm, reads=["brow", "ones_r"], writes=["@PA", "@PB"])
            p.emit("vector", lambda v: v.tensor_copy(dst_s[:, 0:min(S, 1024)], PA[:, 0:min(S, 1024)]),
                   reads=["@PA"], writes=[key])
            if S > 1024:
                p.emit("vector", lambda v: v.tensor_copy(dst_s[:, 1024:S], PB[:, 0:S - 1024]),
                       reads=["@PB"], writes=[key])
        bc_row(mb_s, "mb")
        p.emit("gpsimd", lambda e: e.dma_start(out=bw_s[:], in_=bw_d),
               reads=["mb"], writes=["bw"], dma=True, dsem="c4")
        p.emit("vector", lambda v: v.tensor_scalar(srow2[0:1, 0:half], PC[0:1, 0:half], ck("invD", 1), None, ALU.mult),
               reads=["@PC", "consts"], writes=["srow2"])
        p.emit("vector", lambda v: v.tensor_scalar(srow2[0:1, half:S], PD[0:1, 0:half], ck("invD", 1), None, ALU.mult),
               reads=["@PD", "consts"], writes=["srow2"])
        p.emit("vector", lambda v: v.tensor_mul(srow3[0:1, :], srow[0:1, :], srow[0:1, :]),
               reads=["srow", "sq0", "sq1"], writes=["srow3", "sq0"])
        p.emit("vector", lambda v: v.tensor_sub(srow2[0:1, :], srow2[0:1, :], srow3[0:1, :]),
               reads=["srow2", "srow3"], writes=["srow2"])
        p.emit("scalar", lambda s: s.activation(srow2[0:1, :], srow2[0:1, :], AF.Ln, bias=ck("eps", 1)),
               reads=["srow2", "consts"], writes=["srow2"])
        p.emit("scalar", lambda s: s.activation(srow2[0:1, :], srow2[0:1, :], AF.Exp,
                                                bias=ck("zero", 1), scale=ck("neghalf", 1)),
               reads=["srow2", "consts"], writes=["srow2"])

        def emit_sub(c):
            def sub_m(v, c=c):
                return v.tensor_sub(xn(c), xn(c), mb_s)
            p.emit("vector", sub_m, reads=["mb", f"xn{c}"], writes=[f"xn{c}"])

        NSUB0 = min(4, nD)
        for c in range(NSUB0):
            emit_sub(c)
        p.emit("vector", lambda v: v.tensor_copy(brow[0:1, 0:S], srow2[0:1, :]),
               reads=["srow2"], writes=["brow"])
        bc_row(rb_s, "rb")
        for c in range(NSUB0, nD):
            emit_sub(c)

        raccs = []
        for c in range(nD):
            def mul_r(v, c=c):
                return v.tensor_mul(xn(c), xn(c), rb_s)
            p.emit("vector", mul_r, reads=["rb", f"xn{c}"], writes=[f"xn{c}"])

            NTC = SO // 128
            rpar = c % 2

            def routerc(t, c=c, rpar=rpar, NTC=NTC):
                last = None
                for tc in range(NTC):
                    last = t.matmul(PC[:, rpar * NTC * E + tc * E:rpar * NTC * E + (tc + 1) * E],
                                    lhsT=xn(c)[:, tc * 128:(tc + 1) * 128],
                                    rhs=wr_s[:, c * E:(c + 1) * E],
                                    start=True, stop=True)
                return last
            p.emit("tensor", routerc, reads=[f"xn{c}", "wr"], writes=["@PC"])

            def kmm0(t, c=c):
                last = None
                st, sp = (c == 0), (c == nD - 1)
                for (o0, w) in _slices(half):
                    last = t.matmul(PA[:, o0:o0 + w], lhsT=wk_s[:, c * dh:(c + 1) * dh],
                                    rhs=xn(c)[:, o0:o0 + w], start=st, stop=sp)
                    last = t.matmul(PB[:, o0:o0 + w], lhsT=wk_s[:, c * dh:(c + 1) * dh],
                                    rhs=xn(c)[:, half + o0:half + o0 + w], start=st, stop=sp)
                for (o0, w) in _slices(SO):
                    last = t.matmul(PD[:, o0:o0 + w], lhsT=wq_s[:, c * dh:(c + 1) * dh],
                                    rhs=xn(c)[:, o0:o0 + w], start=st, stop=sp)
                return last
            p.emit("tensor", kmm0, reads=["wk0", "wq0", f"xn{c}", "@PA", "@PB", "@PD"],
                   writes=["@PA", "@PB", "@PD"])

            def racc(v, c=c, rpar=rpar, NTC=NTC):
                pcs = PC[:, rpar * NTC * E:(rpar + 1) * NTC * E]
                if c == 0:
                    return v.tensor_copy(lg_s, pcs)
                return v.tensor_add(lg_s, lg_s, pcs)
            if c >= 1:
                p.emit("vector", raccs[c - 1], reads=["@PC", "lg"], writes=["lg"])
            raccs.append(racc)
        p.emit("vector", raccs[nD - 1], reads=["@PC", "lg"], writes=["lg"])
        p.emit("vector", lambda v: v.engine_nop(),
               reads=[f"xn{c}" for c in range(nD)], writes=["xn"])

        ar.pop()

        def emit_mask_dve():
            for tc in range(SO // 128):
                def rmax(v, tc=tc):
                        return v.reduce_max(mx_s[:, tc:tc + 1], lg_s[:, tc * E:(tc + 1) * E], axis=AX.X)
                p.emit("vector", rmax, reads=["lg"], writes=["mx"])

                def mkmask(v, tc=tc):
                        return v.tensor_scalar(mtm_s[:, tc * E:(tc + 1) * E], lg_s[:, tc * E:(tc + 1) * E],
                                                       mx_s[:, tc:tc + 1], None, ALU.is_equal)
                p.emit("vector", mkmask, reads=["lg", "mx"], writes=["masktm"])

        def emit_mask_pe():
            def masktr(t):
                last = None
                for tc in range(SO // 128):
                        last = t.transpose(PAb[0:E, tc * 128:(tc + 1) * 128],
                                                   mtm_s[:, tc * E:(tc + 1) * E], eye_s[:])
                return last
            p.emit("tensor", masktr, reads=["masktm", "eye"], writes=["@PA"])
            p.emit("vector", lambda v: v.tensor_copy(mfm_s[0:E, 0:SO], PAb[0:E, 0:SO]),
                   reads=["@PA"], writes=["maskfm"])

            def maskrep(t):
                last = None
                for (o0, w) in _slices(SO):
                        last = t.matmul(PB[0:ER, o0:o0 + w], lhsT=r2_s[:],
                                                rhs=mfm_s[0:E, o0:o0 + w], start=True, stop=True)
                return last
            p.emit("tensor", maskrep, reads=["maskfm", "r2", "@PB"], writes=["@PB"])
            p.emit("vector", lambda v: v.tensor_copy(mask16[0:ER, 0:SO], PB[0:ER, 0:SO]),
                   reads=["@PB"], writes=["mask16"])

        ar.push()
        k_sb = ar.alloc("k_b", S, BF16)
        vf_sb = ar.alloc("vf_b", S, BF16)
        vt_sb = vf_sb
        q_sb = ar.alloc("q_b", SO, BF16)
        p_sb = ar.alloc("p_b", 2 * SO, BF16)
        acc_s = ar.alloc("acc_b", nD * SO, BF16)
        wb_s = ar.alloc("wb_b", 2 * W, BF16)
        hm_s = ar.alloc("hm_b", SO, BF16)
        x2_s = ar.alloc("x2_b", 2 * SO, F32)
        PDb = PD[:].bitcast(BF16)

        def acc(j):
            return acc_s[:, j * SO:(j + 1) * SO]

        PSBT = (PA, PB)
        PSKEY = ("@PA", "@PB")

        def kvmm(t, ws, par, lo, hi):
            last = None
            for (o0, w) in _slices(S, min(NS, half)):
                tgt = (lo if o0 < half else hi)
                for c in range(nD):
                    last = t.matmul(tgt[:, (o0 % half):(o0 % half) + w],
                                    lhsT=ws[:, par * W + c * dh:par * W + (c + 1) * dh],
                                    rhs=xn(c)[:, o0:o0 + w],
                                    start=(c == 0), stop=(c == nD - 1))
            return last

        def finalize_a(u):
            sp = u % 2

            def pssum(t, sp=sp):
                last = None
                for (o0, w) in _slices(SO):
                    last = t.matmul(PD[0:1, o0:o0 + w], lhsT=ones_c[:],
                                    rhs=sum_bf[:, sp * SO + o0:sp * SO + o0 + w],
                                    start=True, stop=True)
                return last
            p.emit("vector", (lambda v, u=u: v.tensor_copy(attn_s[:, u * SO:(u + 1) * SO], PC[:, 0:SO])),
                   reads=["@PC"], writes=["attn"])
            p.emit("tensor", pssum, reads=[f"sumacc{sp}", "ones_c", "@PD"], writes=["@PD"])
            p.emit("scalar", lambda s: s.activation(srowb[0:1, 0:SO], PD[0:1, 0:SO], AF.Ln, bias=ck("zero", 1)),
                   reads=["@PD", "consts"], writes=["srowB"])
            p.emit("scalar", lambda s: s.activation(srowb[0:1, 0:SO], srowb[0:1, 0:SO], AF.Exp,
                                                    bias=ck("zero", 1), scale=ck("mone", 1)),
                   reads=["srowB", "consts"], writes=["srowB"])
            p.emit("vector", lambda v: v.tensor_copy(browb[0:1, 0:SO], srowb[0:1, 0:SO]),
                   reads=["srowB"], writes=["browB"])

        def finalize_b(u):
            def bcrec(t):
                last = None
                for (o0, w) in _slices(SO):
                    last = t.matmul(PB[:, o0:o0 + w], lhsT=ones_r[:],
                                    rhs=browb[0:1, o0:o0 + w], start=True, stop=True)
                return last
            p.emit("tensor", bcrec, reads=["browB", "ones_r", "@PB"], writes=["@PB"])
            p.emit("vector", lambda v: v.tensor_copy(recipb, PB[:, 0:SO]),
                   reads=["@PB"], writes=["recipb"])

            def norm(v, u=u):
                a = attn_s[:, u * SO:(u + 1) * SO]
                return v.tensor_mul(a, a, recipb)
            p.emit("vector", norm, reads=["recipb", "attn"], writes=["attn"])

        def scores_f(t, tkc):
            tgt = PSBT[tkc % 2]
            last = None
            for (o0, w) in _slices(SO):
                last = t.matmul(tgt[:, o0:o0 + w],
                                lhsT=k_sb[:, tkc * 128:(tkc + 1) * 128],
                                rhs=q_sb[:, o0:o0 + w], start=True, stop=True)
            return last

        def expev_f(s, tkc):
            src = PSBT[tkc % 2]
            return s.activation(p_sb[:, (tkc % 2) * SO:(tkc % 2 + 1) * SO],
                                src[:, 0:SO], AF.Exp, bias=ck("shift"), scale=ck("scale"))

        def collapse_f(v, tkc, u):
            dst = sum_bf[:, (u % 2) * SO:(u % 2 + 1) * SO]
            pc_ = p_sb[:, (tkc % 2) * SO:(tkc % 2 + 1) * SO]
            if tkc == 0:
                return v.tensor_copy(dst, pc_)
            return v.tensor_add(dst, dst, pc_)

        def av_f(t, tkc):
            last = None
            for (o0, w) in _slices(SO):
                last = t.matmul(PC[:, o0:o0 + w],
                                lhsT=vt_sb[:, tkc * 128:(tkc + 1) * 128],
                                rhs=p_sb[:, (tkc % 2) * SO + o0:(tkc % 2) * SO + o0 + w],
                                start=(tkc == 0), stop=(tkc == nT - 1))
            return last

        for h in range(H):
            par = h % 2
            kwq, kwk, kwv = f"wq{par}", f"wk{par}", f"wv{par}"
            thr = ["mb"] if h <= 2 else []
            if h > 0:
                p.emit("gpsimd", wslice_dma(wq_s[:, par * W:(par + 1) * W], wq_d, h * dh, dh),
                       reads=thr, writes=[kwq], dma=True, dsem=f"wq{par}")
                p.emit("gpsimd", wslice_dma(wk_s[:, par * W:(par + 1) * W], wk_d, h * dh, dh),
                       reads=thr, writes=[kwk], dma=True, dsem=f"wk{par}")
                p.emit("gpsimd", wslice_dma(wv_s[:, par * W:(par + 1) * W], wv_d, h * dh, dh),
                       reads=thr, writes=[kwv], dma=True, dsem=f"wv{par}")
            kwb = f"wb{par}"
            p.emit("gpsimd", wslice_dma(wb_s[:, par * W:(par + 1) * W], wb_d, h * 128, 128),
                   reads=thr, writes=[kwb], dma=True, dsem=f"wb{par}")

            if h > 0:
                p.emit("tensor", (lambda t, par=par, f=kvmm: f(t, wk_s, par, PA, PB)),
                       reads=[kwk, "xn"], writes=["@PA", "@PB"])
            p.emit("scalar", lambda s: s.copy(k_sb[:, 0:half], PA[:, 0:half]),
                   reads=["@PA"], writes=["ksb"])
            p.emit("scalar", lambda s: s.copy(k_sb[:, half:S], PB[:, 0:half]),
                   reads=["@PB"], writes=["ksb"])
            if h == 0:
                p.emit("scalar", lambda s: s.copy(q_sb, PD[:, 0:SO]),
                       reads=["@PD"], writes=["qsb"])
                emit_mask_dve()
            if h > 0:
                finalize_a(h - 1)

            if h > 0:
                def qmm(t, par=par):
                    last = None
                    for (o0, w) in _slices(SO):
                        for c in range(nD):
                            last = t.matmul(PA[:, o0:o0 + w],
                                            lhsT=wq_s[:, par * W + c * dh:par * W + (c + 1) * dh],
                                            rhs=xn(c)[:, o0:o0 + w],
                                            start=(c == 0), stop=(c == nD - 1))
                    return last
                p.emit("tensor", qmm, reads=[kwq, "xn", "@PA"], writes=["@PA"])
                p.emit("scalar", lambda s: s.copy(q_sb, PA[:, 0:SO]),
                       reads=["@PA"], writes=["qsb"])

            def vlo(t, par=par):
                last = None
                for c in range(nD):
                    for (o0, w) in _slices(half):
                        last = t.matmul(PC[:, o0:o0 + w],
                                        lhsT=wv_s[:, par * W + c * dh:par * W + (c + 1) * dh],
                                        rhs=xn(c)[:, o0:o0 + w],
                                        start=(c == 0), stop=(c == nD - 1))
                return last
            p.emit("tensor", vlo, reads=[kwv, "xn", "@PC"], writes=["@PC"])
            p.emit("scalar", lambda s: s.copy(vf_sb[:, 0:half], PC[:, 0:half]),
                   reads=["@PC"], writes=["vflo"])

            def vhi(t, par=par):
                last = None
                for c in range(nD):
                    for (o0, w) in _slices(half):
                        last = t.matmul(PD[:, o0:o0 + w],
                                        lhsT=wv_s[:, par * W + c * dh:par * W + (c + 1) * dh],
                                        rhs=xn(c)[:, half + o0:half + o0 + w],
                                        start=(c == 0), stop=(c == nD - 1))
                return last
            p.emit("tensor", vhi, reads=[kwv, "xn", "@PD"], writes=["@PD"])
            p.emit("scalar", lambda s: s.copy(vf_sb[:, half:S], PD[:, 0:half]),
                   reads=["@PD"], writes=["vfhi"])
            if h == 0:
                emit_mask_pe()

            def vtlo(t):
                last = None
                for c in range(nT // 2):
                    last = t.transpose(PCb[:, c * 128:(c + 1) * 128],
                                       vf_sb[:, c * 128:(c + 1) * 128], eye_s[:])
                return last
            p.emit("tensor", vtlo, reads=["vflo", "eye", "@PC"], writes=["@PC"])
            p.emit("scalar", lambda s: s.copy(vt_sb[:, 0:half], PCb[:, 0:half]),
                   reads=["@PC"], writes=["vt", "vflo"])
            if h > 0:
                finalize_b(h - 1)

            def vthi(t):
                last = None
                for c in range(nT // 2, nT):
                    last = t.transpose(PDb[:, c * 128:(c + 1) * 128],
                                       vf_sb[:, c * 128:(c + 1) * 128], eye_s[:])
                return last
            p.emit("tensor", vthi, reads=["vfhi", "eye", "@PD"], writes=["@PD"])

            NSL = len(_slices(SO))
            BITEMS = [(sl, c) for sl in range(NSL) for c in range(nD)]
            BSH = min(2, nT - 1)
            M0 = (NSL * nD + (nT - BSH) - 1) // (nT - BSH)

            def basemm(t, tkc, par=par, h=h, BITEMS=BITEMS, M0=M0):
                last = None
                for (sl, c) in BITEMS[(tkc - BSH) * M0:(tkc - BSH + 1) * M0]:
                    (o0, w) = _slices(SO)[sl]
                    last = t.matmul(PD[:, o0:o0 + w],
                                    lhsT=wb_s[:, par * W + c * dh:par * W + (c + 1) * dh],
                                    rhs=xn(c)[:, o0:o0 + w],
                                    start=(c == 0), stop=(c == nD - 1))
                return last

            for tkc in range(nT):
                p.emit("tensor", (lambda t, tkc=tkc, f=scores_f: f(t, tkc)),
                       reads=["ksb", "qsb", PSKEY[tkc % 2]], writes=[PSKEY[tkc % 2]])
                p.emit("scalar", (lambda s, tkc=tkc, f=expev_f: f(s, tkc)),
                       reads=[PSKEY[tkc % 2], "consts"], writes=[f"p{tkc % 2}"])
                p.emit("vector", (lambda v, tkc=tkc, h=h, f=collapse_f: f(v, tkc, h)),
                       reads=[f"p{tkc % 2}", f"sumacc{h % 2}"], writes=[f"sumacc{h % 2}"])
                if tkc >= 1:
                    p.emit("tensor", (lambda t, tkc=tkc, f=av_f: f(t, tkc - 1)),
                           reads=[f"p{(tkc - 1) % 2}", "vt"], writes=["@PC"])
                if tkc == 1:
                    p.emit("scalar", lambda s: s.copy(vt_sb[:, half:S], PDb[:, half:S]),
                           reads=["@PD"], writes=["vt", "vfhi"])
                if tkc >= BSH and BITEMS[(tkc - BSH) * M0:(tkc - BSH + 1) * M0]:
                    p.emit("tensor", (lambda t, tkc=tkc, f=basemm: f(t, tkc)),
                           reads=[kwb, "xn", "@PD"], writes=["@PD"])
            p.emit("tensor", (lambda t, f=av_f: f(t, nT - 1)),
                   reads=[f"p{(nT - 1) % 2}", "vt"], writes=["@PC"])

            def baev(v, h=h):
                return v.tensor_scalar(acc(h), PD[:, 0:SO], bb_s[:, h:h + 1], None, ALU.add)
            p.emit("vector", baev, reads=["@PD", "bb"], writes=[f"acc{h}"])

        at_s = wq_s[:, 0:W]

        p.emit("gpsimd", wslice_dma(at_s, at_d, 0, ER),
               writes=["at", "wq0"], dma=True, dsem="at", fresh_ok=True)

        def hmm_(t):
            last = None
            for (o0, w) in _slices(SO):
                for c in range(nD):
                    last = t.matmul(PA[0:ER, o0:o0 + w],
                                    lhsT=at_s[:, c * ER:(c + 1) * ER],
                                    rhs=xn(c)[:, o0:o0 + w],
                                    start=(c == 0), stop=(c == nD - 1))
            return last
        finalize_a(H - 1)
        p.emit("tensor", hmm_, reads=["at", "xn", "@PA"], writes=["@PA"])
        finalize_b(H - 1)
        p.emit("vector", lambda v: v.tensor_mul(hm_s[0:ER, 0:SO], PA[0:ER, 0:SO], mask16[0:ER, 0:SO]),
               reads=["@PA", "mask16"], writes=["hm"])

        last_store = None
        WOBUF = (wk_s[:, 0:W], wv_s[:, 0:W], wk_s[:, W:2 * W], wv_s[:, W:2 * W])
        WOPRV = ("wk0", "wv0", "wk1", "wv1")
        for j in range(nD):
            par = j % 2
            p4 = j % 4
            kw = f"wo{p4}"
            p.emit("gpsimd", wslice_dma(WOBUF[p4], wo_d, j * 128, 128),
                   writes=[kw, WOPRV[p4]], dma=True, dsem=f"wx{p4}", fresh_ok=True)
            OPP = (PC, PD)[j % 2]
            OPK = ("@PC", "@PD")[j % 2]

            def opmm(t, p4=p4, OPP=OPP):
                last = None
                wb_ = WOBUF[p4]
                for (o0, w) in _slices(SO):
                    for c in range(H):
                        last = t.matmul(OPP[:, o0:o0 + w],
                                        lhsT=wb_[:, c * 128:(c + 1) * 128],
                                        rhs=attn_s[:, c * SO + o0:c * SO + o0 + w],
                                        start=(c == 0), stop=(c == H - 1))
                return last
            p.emit("tensor", opmm, reads=[kw, "attn", OPK], writes=[OPK])

            def opev(v, j=j, OPP=OPP):
                return v.scalar_tensor_tensor(acc(j), OPP[:, 0:SO], ck("one"),
                                              acc(j), ALU.mult, ALU.add)
            p.emit("vector", opev, reads=[OPK, "consts", f"acc{j}"], writes=[f"acc{j}"])

            DPP = (PA, PB)[j % 2]
            DPK = ("@PA", "@PB")[j % 2]

            def demm(t, j=j, DPP=DPP):
                last = None
                for (o0, w) in _slices(SO):
                    last = t.matmul(DPP[:, o0:o0 + w],
                                    lhsT=bw_s[:, j * 128:(j + 1) * 128],
                                    rhs=hm_s[0:ER, o0:o0 + w], start=True, stop=True)
                return last
            p.emit("tensor", demm, reads=["bw", "hm", DPK], writes=[DPK])

            def deev(v, j=j, DPP=DPP):
                return v.scalar_tensor_tensor(acc(j), DPP[:, 0:SO], ck("one"),
                                              acc(j), ALU.mult, ALU.add)
            p.emit("vector", deev, reads=[DPK, "consts", f"acc{j}"], writes=[f"acc{j}"])

            kx = f"x2{par}"

            def dldx3(e, j=j, par=par):
                return e.dma_start(out=x2_s[:, par * SO:(par + 1) * SO],
                                   in_=x_d[j * 128:(j + 1) * 128, 0:SO])
            p.emit("sync", dldx3, writes=[kx], dma=True, dsem=f"x{par}")

            def resadd(v, j=j, par=par):
                xs = x2_s[:, par * SO:(par + 1) * SO]
                return v.tensor_add(xs, xs, acc(j))
            p.emit("vector", resadd, reads=[kx, f"acc{j}"], writes=[kx])

            def store(e, j=j, par=par):
                return e.dma_start(out=y_d[j * 128:(j + 1) * 128, :],
                                   in_=x2_s[:, par * SO:(par + 1) * SO])
            last_store = p.emit("scalar", store, reads=[kx], writes=["y"], dma=True, dsem="out")
        ar.pop()

        p.wait("scalar", last_store)
        p.replay(nc, top)
    return nc


def _prep_shared(Wq, Wk, Wv, Wo, W_base, b_base, router_W, A, Bw, D, E, R):
    nD = D // 128
    ER = E * R
    return {
        "wq": np.ascontiguousarray(np.asarray(Wq, np.float32).T),
        "wk": np.ascontiguousarray(np.asarray(Wk, np.float32).T),
        "wv": np.ascontiguousarray(np.asarray(Wv, np.float32).T),
        "wo": np.ascontiguousarray(np.asarray(Wo, np.float32).T),
        "wb": np.ascontiguousarray(np.asarray(W_base, np.float32).T),
        "wr": np.ascontiguousarray(
            np.asarray(router_W, np.float32).T.reshape(nD, 128, E)
            .transpose(1, 0, 2).reshape(128, nD * E)),
        "at": np.ascontiguousarray(np.asarray(A, np.float32).reshape(ER, D).T),
        "bw": np.ascontiguousarray(
            np.asarray(Bw, np.float32).transpose(0, 2, 1).reshape(ER, D)),
        "bb": np.ascontiguousarray(np.asarray(b_base, np.float32).reshape(nD, 128).T),
        "r2": np.ascontiguousarray(np.kron(np.eye(E, dtype=np.float32),
                                           np.ones((1, R), np.float32)) * 2.0),
        "eye": np.eye(128, dtype=ml_dtypes.bfloat16),
    }


def kernel(L0_attn_hidden_states, L0_attn_Wq, L0_attn_Wk, L0_attn_Wv,
           L0_attn_Wo, L0_ffn_W_base, L0_ffn_b_base, L0_ffn_router_W,
           L0_ffn_A, L0_ffn_Bw):
    D, S, SO, H, E, R = 2048, 2048, 1024, 16, 8, 16
    x = np.asarray(L0_attn_hidden_states, dtype=np.float32)
    B = x.shape[0]
    x_fm = np.ascontiguousarray(x.reshape(B * S, D).T)
    shared = _prep_shared(L0_attn_Wq, L0_attn_Wk, L0_attn_Wv, L0_attn_Wo,
                          L0_ffn_W_base, L0_ffn_b_base, L0_ffn_router_W,
                          L0_ffn_A, L0_ffn_Bw, D, E, R)
    in_maps = []
    for core in range(2 * B):
        b, g = divmod(core, 2)
        seq = x_fm[:, b * S:(b + 1) * S]
        own = seq[:, g * SO:(g + 1) * SO]
        part = np.concatenate([seq[:, :g * SO], seq[:, (g + 1) * SO:]], axis=1)
        m = dict(shared)
        m["x"] = np.ascontiguousarray(np.concatenate([own, part], axis=1))
        in_maps.append(m)

    nc = build(D=D, S=S, SO=SO, H=H, E=E, R=R)
    res = run_bass_kernel_spmd(nc, in_maps, core_ids=list(range(2 * B)))
    out_fm = np.empty((D, B * S), dtype=np.float32)
    for core in range(2 * B):
        b, g = divmod(core, 2)
        out_fm[:, b * S + g * SO:b * S + (g + 1) * SO] = res.results[core]["y"]
    return np.ascontiguousarray(out_fm.T).reshape(B, S, D)
